# version 05 (final)
# baseline (speedup 1.0000x reference)
import contextlib
import numpy as np
import ml_dtypes
import concourse.bass as bass
import concourse.mybir as mybir
from concourse.bass_utils import run_bass_kernel_spmd

F32 = mybir.dt.float32
BF16 = mybir.dt.bfloat16
AF = mybir.ActivationFunctionType
ALU = mybir.AluOpType
AX = mybir.AxisListType.X

T = 2048
TS = 2176
HOFF = 3
NT = 17
DM = 1024
DIN = 2048
H, P, N, Q = 32, 64, 64, 128
NCH = 16
DPROJ = 4352
NCT = 34
CCT = 17
I_FFN = 2816
ICT = 22
EPS = 1e-5
MAGIC = float(np.float32(1.5 * 2 ** 23))
NEGM = -30000.0


class Sem:
    def __init__(self, h):
        self.h = h
        self.n = 0


class Rec:
    ENGS = ("sync", "gpsimd", "vector", "scalar", "tensor")

    def __init__(self, nc, es, semnames):
        self.nc = nc
        self.s = {k: [] for k in Rec.ENGS}
        self.sem = {nm: Sem(es.enter_context(nc.semaphore(nm))) for nm in semnames}

    def op(self, eng, fn, inc=None, k=1):
        if inc is not None:
            sem = self.sem[inc]
            sem.n += k
            self.s[eng].append(lambda e, f=fn, s=sem.h, kk=k: f(e).then_inc(s, kk))
        else:
            self.s[eng].append(fn)

    def dma(self, eng, out, in_, inc="ld"):
        self.op(eng, lambda e, o=out, i=in_: e.dma_start(out=o, in_=i), inc=inc, k=16)

    def wait(self, eng, semname, at):
        sem = self.sem[semname]
        self.s[eng].append(lambda e, s=sem.h, vv=at: e.wait_ge(s, vv))

    def mark(self, semname):
        return self.sem[semname].n

    def fop(self, eng, fn, inc=None, k=1):
        if inc is not None:
            self.op(eng, fn, inc=inc, k=k)
            self.wait(eng, inc, self.mark(inc))
        else:
            self.op(eng, fn, inc="fe")
            self.wait(eng, "fe", self.mark("fe"))

    def barrier(self):
        for eng in Rec.ENGS:
            self.op(eng, lambda e: e.nop(), inc="bar")
        tot = self.sem["bar"].n
        for eng in Rec.ENGS:
            self.wait(eng, "bar", tot)

    def build(self):
        with self.nc.Block() as b:
            b.sync(lambda e: [f(e) for f in self.s["sync"]])
            b.gpsimd(lambda e: [f(e) for f in self.s["gpsimd"]])
            b.vector(lambda e: [f(e) for f in self.s["vector"]])
            b.scalar(lambda e: [f(e) for f in self.s["scalar"]])
            b.tensor(lambda e: [f(e) for f in self.s["tensor"]])



def build_l1():
    nc = bass.Bass("TRN2", target_bir_lowering=False, debug=False)
    es = contextlib.ExitStack()

    u = nc.dram_tensor("u", [TS, DM], F32, kind="ExternalInput").ap()
    w_inT = nc.dram_tensor("w_inT", [2, DM, DPROJ], F32, kind="ExternalInput").ap()
    diag = nc.dram_tensor("diag", [2, CCT, 4, 128, 128], F32, kind="ExternalInput").ap()
    convb = nc.dram_tensor("convb", [2, 128, CCT], F32, kind="ExternalInput").ap()
    dtb_a = nc.dram_tensor("dtb_a", [2, 32, 2], F32, kind="ExternalInput").ap()
    ibf = nc.dram_tensor("ibf", [128, 128], BF16, kind="ExternalInput").ap()

    o_sz = nc.dram_tensor("o_sz", [2, 16, 128, T], BF16, kind="ExternalOutput").ap()
    o_xt = nc.dram_tensor("o_xt", [2, NCH, 128, DIN], BF16, kind="ExternalOutput").ap()
    o_bc = nc.dram_tensor("o_bc", [2, 128, T], BF16, kind="ExternalOutput").ap()
    o_bct = nc.dram_tensor("o_bct", [2, NCH, 128, 128], BF16, kind="ExternalOutput").ap()
    o_dwv = nc.dram_tensor("o_dwv", [2, 3, 32, T], BF16, kind="ExternalOutput").ap()
    o_pl = nc.dram_tensor("o_pl", [2, 2, 32, T], BF16, kind="ExternalOutput").ap()
    o_dec = nc.dram_tensor("o_dec", [2, 32, NCH], F32, kind="ExternalOutput").ap()
    o_dectot = nc.dram_tensor("o_dectot", [2, 32], F32, kind="ExternalOutput").ap()
    o_gsum = nc.dram_tensor("o_gsum", [2, 64, DIN], F32, kind="ExternalOutput").ap()
    o_maxp = nc.dram_tensor("o_maxp", [1, 1], F32, kind="ExternalOutput").ap()

    a = nc.alloc_sbuf_tensor
    s_ibf = a("s_ibf", [128, 128], BF16)
    s_u = a("s_u", [128, 2 * DM], F32)
    s_c = a("s_c", [128, DM], F32)
    s_st = a("s_st", [128, 8], F32)
    s_xnb = a("s_xnb", [128, DM], BF16)
    s_xnT = a("s_xnT", [128, 8 * TS], BF16)
    s_mx = a("s_mx", [128, 2], F32)
    s_mxf = a("s_mxf", [1, 128], F32)
    s_w = a("s_w", [128, 2 * 1024], BF16)
    s_dg = a("s_dg", [128, 4 * 512], BF16)
    s_cb = a("s_cb", [128, CCT], F32)
    s_ev = a("s_ev", [128, 2 * TS], BF16)
    s_xlin = a("s_xlin", [128, 2 * TS], BF16)
    s_xev = a("s_xev", [128, 16 * T], BF16)
    s_bcst = a("s_bcst", [128, T], BF16)
    s_A = a("s_A", [32, TS], F32)
    s_B = a("s_B", [32, T], F32)
    s_dt = a("s_dt", [32, T], F32)
    s_P = a("s_P", [32, T], F32)
    s_PL = a("s_PL", [32, T], F32)
    s_dba = a("s_dba", [32, 4], F32)
    s_pp = a("s_pp", [32, NCH], F32)
    s_dc = a("s_dc", [32, NCH], F32)
    s_bf1 = a("s_bf1", [32, T], BF16)
    s_plhl = a("s_plhl", [64, T], BF16)
    s_wgb = a("s_wgb", [32, T], BF16)
    s_xtile = a("s_xtile", [128, DIN], BF16)
    s_bct = a("s_bct", [128, 128], BF16)
    s_wgt = a("s_wgt", [128, 32], BF16)
    s_wgx = a("s_wgx", [128, DIN], BF16)
    s_gs = a("s_gs", [64, 1024], F32)
    s_cst = a("s_cst", [128, 2], F32)

    ps = nc.alloc_psum_tensor("ps", [128, 4096], F32)
    psb = ps[:].bitcast(BF16)

    r = Rec(nc, es, ["ld", "wl", "ts", "vs", "as_", "st", "evs", "evv", "evc", "bar", "gp", "fe"])

    TG = [(0, 512), (512, 512), (1024, 512), (1536, 512), (2048, 128)]

    def ts_inproj_end(ct):
        if ct < 16:
            return 5 * (ct + 1)
        if ct <= 32:
            return 80 + 9 * (ct - 16) + 5
        return 80 + 9 * 17 + 5

    def ts_at(ct):
        if ct < 16:
            return 5 * (ct + 1)
        if ct <= 32:
            return 80 + 9 * (ct - 16 + 1)
        return 80 + 9 * 17 + 5

    with es:
        r.dma("sync", s_ibf[:], ibf)
        r.dma("sync", s_cb[:], convb[0])
        r.dma("sync", s_dba[:, 0:2], dtb_a[0])
        ld0 = r.mark("ld")
        r.op("vector", lambda e: e.memset(s_cst[:, 0:1], EPS))
        r.op("vector", lambda e: e.memset(s_cst[:, 1:2], 1.0))

        VPT = 5
        vs0, as0, ts0 = r.mark("vs"), r.mark("as_"), r.mark("ts")
        for i in range(NT):
            if i >= 2:
                r.wait("sync", "vs", vs0 + VPT * (i - 1))
            r.dma("sync", u_dst := s_u[:, (i % 2) * DM:(i % 2 + 1) * DM], u[128 * i:128 * (i + 1), :])
        for i in range(NT):
            sl = (i % 2) * DM
            r.wait("vector", "ld", ld0 + 16 * min(i + 2, NT))
            if i == 0:
                r.op("vector", lambda e: e.memset(s_mx[:], 0.0))
                r.op("vector", lambda e: e.memset(s_dba[:, 2:4], 0.0))
            r.fop("vector", lambda e, sl=sl: e.tensor_reduce(s_st[:, 0:1], s_u[:, sl:sl + DM], AX, ALU.add))
            r.fop("vector", lambda e: e.tensor_scalar(s_st[:, 1:2], s_st[:, 0:1], 1.0 / DM, None, ALU.mult))
            r.op("vector", lambda e, sl=sl: e.tensor_scalar(s_c[:], s_u[:, sl:sl + DM], s_st[:, 1:2], None, ALU.subtract),
                 inc="vs")
            r.wait("scalar", "vs", vs0 + VPT * i + 1)
            r.op("scalar", lambda e: e.activation(s_xnb[:], s_c[:], AF.Square, accum_out=s_st[:, 2:3]),
                 inc="as_")
            r.wait("vector", "as_", as0 + 2 * i + 1)
            r.op("vector", lambda e: e.tensor_scalar(s_st[:, 3:4], s_st[:, 2:3], 1.0 / DM, None, ALU.mult),
                 inc="vs")
            r.wait("scalar", "vs", vs0 + VPT * i + 2)
            r.op("scalar", lambda e: e.activation(s_st[:, 4:5], s_st[:, 3:4], AF.Sqrt, bias=s_cst[:, 0:1]), inc="as_")
            r.wait("vector", "as_", as0 + 2 * i + 2)
            r.fop("vector", lambda e: e.reciprocal(s_st[:, 5:6], s_st[:, 4:5]))
            r.op("vector", lambda e: e.tensor_scalar(s_xnb[:], s_c[:], s_st[:, 5:6], None, ALU.mult),
                 inc="vs")
            r.fop("vector", lambda e: e.reduce_max(s_st[:, 6:7], s_c[:], AX, apply_absolute_value=True))
            r.op("vector", lambda e: e.tensor_tensor(s_st[:, 7:8], s_st[:, 6:7], s_st[:, 5:6], ALU.mult))
            r.op("vector", lambda e: e.tensor_tensor(s_mx[:, 0:1], s_mx[:, 0:1], s_st[:, 7:8], ALU.max),
                 inc="vs")
            r.wait("tensor", "vs", vs0 + VPT * i + 3)
            if i >= 1:
                r.wait("tensor", "vs", vs0 + VPT * i)
            for ks in range(8):
                r.op("tensor",
                     lambda e, ks=ks: e.transpose(psb[:, 4096 + 128 * ks:4096 + 128 * (ks + 1)],
                                                  s_xnb[:, 128 * ks:128 * (ks + 1)], s_ibf[:]),
                     inc="ts" if ks == 7 else None)
            r.wait("vector", "ts", ts0 + i + 1)
            r.op("vector",
                 lambda e, i=i: e.tensor_copy(
                     s_xnT[:].rearrange("p (k t) -> p k t", k=8)[:, :, 128 * i:128 * (i + 1)],
                     psb[:, 4096:4096 + 1024].rearrange("p (k t) -> p k t", k=8)),
                 inc="vs")
        r.wait("gpsimd", "vs", r.mark("vs"))
        r.dma("gpsimd", s_mxf[:], s_mx[:, 0:1], inc="gp")
        r.wait("vector", "gp", r.mark("gp"))
        r.fop("vector", lambda e: e.reduce_max(s_mxf[:, 0:1], s_mxf[:], AX))
        r.op("vector", lambda e: e.nop(), inc="vs")
        r.wait("sync", "vs", r.mark("vs"))
        r.dma("sync", o_maxp, s_mxf[:, 0:1], inc="st")
        r.barrier()

        for d in range(2):
            if d == 1:
                r.dma("sync", s_cb[:], convb[1])
                r.dma("sync", s_dba[:, 0:2], dtb_a[1])
                r.wait("sync", "ld", r.mark("ld"))
                r.barrier()

            ts0 = r.mark("ts")
            evs0, evv0, evc0, st0 = r.mark("evs"), r.mark("evv"), r.mark("evc"), r.mark("st")
            wlW, wlD = {}, {}
            for ct in range(NCT):
                if ct >= 2:
                    r.wait("gpsimd", "ts", ts0 + ts_inproj_end(ct - 2))
                r.dma("gpsimd",
                      s_w[:, (ct % 2) * 1024:(ct % 2 + 1) * 1024],
                      w_inT[d].rearrange("(a p) c -> p a c", p=128)[:, :, 128 * ct:128 * (ct + 1)],
                      inc="wl")
                wlW[ct] = r.mark("wl")
                cc = ct - 13
                if 0 <= cc < CCT:
                    if cc >= 4:
                        r.wait("gpsimd", "ts", ts0 + ts_at(16 + cc - 4))
                    r.dma("gpsimd",
                          s_dg[:, (cc % 4) * 512:(cc % 4 + 1) * 512].rearrange("p (k c) -> p k c", k=4),
                          diag[d, cc].rearrange("k p c -> p k c"),
                          inc="wl")
                    wlD[cc] = r.mark("wl")

            def thrW(ct):
                if 12 <= ct <= 15:
                    return wlD[ct - 12]
                return wlW[min(ct + 1, NCT - 1)]

            def thrD(cc):
                return wlW[min(17 + cc, NCT - 1)]

            for ct in range(NCT):
                r.wait("tensor", "wl", thrW(ct))
                for tg, (t0, tl) in enumerate(TG):
                    if ct < 16:
                        gi = 5 * ct + tg
                        if gi >= 2:
                            r.wait("tensor", "evs", evs0 + gi - 1)
                        po = (gi % 2) * 512
                    else:
                        gi = 5 * (ct - 16) + tg if ct <= 32 else 85 + tg
                        if gi >= 2:
                            r.wait("tensor", "evv", evv0 + gi - 1)
                        po = 1024 + (gi % 2) * 512
                    for ks in range(8):
                        r.op("tensor",
                             lambda e, po=po, tl=tl, ks=ks, ct=ct, t0=t0:
                             e.matmul(ps[:, po:po + tl],
                                      s_w[:, (ct % 2) * 1024 + 128 * ks:(ct % 2) * 1024 + 128 * (ks + 1)],
                                      s_xnT[:].rearrange("p (k t) -> p k t", k=8)[:, ks, t0:t0 + tl],
                                      start=(ks == 0), stop=(ks == 7)),
                             inc="ts" if ks == 7 else None)
                if 16 <= ct <= 32:
                    cc = ct - 16
                    r.wait("tensor", "wl", thrD(cc))
                    r.wait("tensor", "evv", evv0 + 5 * cc + 5)
                    for tg in range(4):
                        gi = 4 * cc + tg
                        if gi >= 2:
                            r.wait("tensor", "evc", evc0 + gi - 1)
                        po = 2048 + (gi % 2) * 512
                        for k in range(4):
                            sh = tg * 512 + (k if d == 0 else 6 - k)
                            r.op("tensor",
                                 lambda e, po=po, cc=cc, k=k, sh=sh:
                                 e.matmul(ps[:, po:po + 512],
                                          s_dg[:, (cc % 4) * 512 + 128 * k:(cc % 4) * 512 + 128 * (k + 1)],
                                          s_xlin[:, (cc % 2) * TS + sh:(cc % 2) * TS + sh + 512],
                                          start=(k == 0), stop=(k == 3)),
                                 inc="ts" if k == 3 else None)

            for ct in range(16):
                for tg, (t0, tl) in enumerate(TG):
                    gi = 5 * ct + tg
                    po = (gi % 2) * 512
                    r.wait("scalar", "ts", ts0 + 5 * ct + tg + 1)
                    if tg == 0 and ct >= 2:
                        r.wait("scalar", "st", st0 + 16 * ct)
                    r.op("scalar",
                         lambda e, po=po, t0=t0, tl=tl, ct=ct:
                         e.activation(s_ev[:, (ct % 2) * TS + t0:(ct % 2) * TS + t0 + tl],
                                      ps[:, po:po + tl], AF.Silu),
                         inc="evs")
            for cc in range(CCT):
                for tg in range(4):
                    gi = 4 * cc + tg
                    po = 2048 + (gi % 2) * 512
                    r.wait("scalar", "ts", ts0 + ts_inproj_end(16 + cc) + tg + 1)
                    if cc < 16:
                        fn = lambda e, cc=cc, tg=tg, po=po: e.activation(
                            s_xev[:, cc * T + tg * 512:cc * T + tg * 512 + 512],
                            ps[:, po:po + 512], AF.Silu, bias=s_cb[:, cc:cc + 1])
                    else:
                        fn = lambda e, tg=tg, po=po, cc=cc: e.activation(
                            s_bcst[:, tg * 512:tg * 512 + 512],
                            ps[:, po:po + 512], AF.Silu, bias=s_cb[:, cc:cc + 1])
                    r.op("scalar", fn, inc="evc")

            for ct in range(16, NCT):
                for tg, (t0, tl) in enumerate(TG):
                    gi = 5 * (ct - 16) + tg if ct <= 32 else 85 + tg
                    po = 1024 + (gi % 2) * 512
                    r.wait("vector", "ts", ts0 + ts_inproj_end(ct) - 5 + tg + 1)
                    if ct <= 32:
                        cc = ct - 16
                        if tg == 0 and cc >= 2:
                            r.wait("vector", "ts", ts0 + ts_at(14 + cc))
                        r.op("vector",
                             lambda e, po=po, t0=t0, tl=tl, cc=cc:
                             e.tensor_copy(s_xlin[:, (cc % 2) * TS + t0:(cc % 2) * TS + t0 + tl],
                                           ps[:, po:po + tl]),
                             inc="evv")
                    else:
                        r.op("vector",
                             lambda e, po=po, t0=t0, tl=tl:
                             e.tensor_copy(s_A[0:32, t0:t0 + tl], ps[0:32, po:po + tl]),
                             inc="evv")

            for ct in range(16):
                r.wait("sync", "evs", evs0 + 5 * ct + 5)
                r.dma("sync", o_sz[d, ct], s_ev[:, (ct % 2) * TS + HOFF:(ct % 2) * TS + HOFF + T], inc="st")
            r.wait("sync", "evc", evc0 + 4 * CCT)
            r.dma("sync", o_bc[d], s_bcst[:], inc="st")
            r.barrier()

            vs0, as0 = r.mark("vs"), r.mark("as_")

            def V(fn, inc=None):
                r.op("vector", fn, inc=inc)

            def A_(fn, inc=None):
                r.op("scalar", fn, inc=inc)

            A_(lambda e: e.activation(s_dba[:, 2:3], s_dba[:, 1:2], AF.Exp), inc="as_")
            r.wait("vector", "as_", r.mark("as_"))
            V(lambda e: e.tensor_scalar(s_dba[:, 2:3], s_dba[:, 2:3], -1.0, None, ALU.mult))
            V(lambda e: e.tensor_scalar(s_dt[:], s_A[0:32, HOFF:HOFF + T], s_dba[:, 0:1], None, ALU.add),
              inc="vs")
            r.wait("scalar", "vs", r.mark("vs"))
            A_(lambda e: e.activation(s_B[:], s_dt[:], AF.Abs), inc="as_")
            A_(lambda e: e.activation(s_B[:], s_B[:], AF.Exp, scale=-1.0), inc="as_")
            A_(lambda e: e.activation(s_B[:], s_B[:], AF.Ln, bias=s_cst[0:32, 1:2]), inc="as_")
            r.wait("vector", "as_", r.mark("as_"))
            V(lambda e: e.tensor_scalar(s_dt[:], s_dt[:], 0.0, None, ALU.max))
            V(lambda e: e.tensor_tensor(s_dt[:], s_dt[:], s_B[:], ALU.add))
            r.fop("vector", lambda e: e.tensor_scalar(s_A[0:32, 0:T], s_dt[:], s_dba[:, 2:3], None, ALU.mult))
            r.fop("vector", lambda e: e.tensor_tensor_scan(s_P[:], s_A[0:32, 0:T],
                                             s_dba[:, 3:4].broadcast_to([32, T]), 0.0, ALU.add, ALU.add))
            V(lambda e: e.nop(), inc="vs")
            if d == 1:
                V(lambda e: e.tensor_scalar(s_B[:], s_P[:], -1.0, s_P[:, T - 1:T], ALU.mult, ALU.add))
                r.fop("vector", lambda e: e.tensor_tensor(s_P[:], s_B[:], s_A[0:32, 0:T], ALU.add))
                V(lambda e: e.nop(), inc="vs")
            if d == 0:
                V(lambda e: e.memset(s_pp[:, 0:1], 0.0))
                r.fop("vector", lambda e: e.tensor_copy(s_pp[:, 1:NCH],
                                          s_P[:].rearrange("h (c q) -> h c q", q=Q)[:, 0:NCH - 1, Q - 1]))
            else:
                V(lambda e: e.memset(s_pp[:, NCH - 1:NCH], 0.0))
                r.fop("vector", lambda e: e.tensor_copy(s_pp[:, 0:NCH - 1],
                                          s_P[:].rearrange("h (c q) -> h c q", q=Q)[:, 1:NCH, 0]))
            V(lambda e: e.tensor_tensor(
                s_PL[:].rearrange("h (c q) -> h c q", q=Q),
                s_P[:].rearrange("h (c q) -> h c q", q=Q),
                s_pp[:].unsqueeze(2).broadcast_to([32, NCH, Q]), ALU.subtract), inc="vs")
            r.wait("scalar", "vs", r.mark("vs"))
            A_(lambda e: e.activation(s_bf1[:], s_PL[:], AF.Exp), inc="as_")
            r.wait("sync", "as_", r.mark("as_"))
            r.dma("sync", o_dwv[d, 2], s_bf1[:], inc="st")
            lastq = Q - 1 if d == 0 else 0
            V(lambda e, lq=lastq: e.tensor_tensor(
                s_B[:].rearrange("h (c q) -> h c q", q=Q),
                s_P[:].rearrange("h (c q) -> h c q", q=Q)[:, :, lq].unsqueeze(2).broadcast_to([32, NCH, Q]),
                s_P[:].rearrange("h (c q) -> h c q", q=Q), ALU.subtract), inc="vs")
            r.wait("scalar", "vs", r.mark("vs"))
            A_(lambda e: e.activation(s_B[:], s_B[:], AF.Exp), inc="as_")
            r.wait("vector", "as_", r.mark("as_"))
            r.wait("vector", "st", r.mark("st"))
            V(lambda e: e.tensor_tensor(s_bf1[:], s_B[:], s_dt[:], ALU.mult), inc="vs")
            r.wait("sync", "vs", r.mark("vs"))
            r.dma("sync", o_dwv[d, 1], s_bf1[:], inc="st")
            totc = T - 1 if d == 0 else 0
            V(lambda e, tc=totc: e.tensor_scalar(s_B[:], s_P[:], -1.0, s_P[:, tc:tc + 1], ALU.mult, ALU.add),
              inc="vs")
            r.wait("scalar", "vs", r.mark("vs"))
            A_(lambda e: e.activation(s_B[:], s_B[:], AF.Exp), inc="as_")
            A_(lambda e, tc=totc: e.activation(s_dba[:, 3:4], s_P[:, tc:tc + 1], AF.Exp), inc="as_")
            r.wait("vector", "as_", r.mark("as_"))
            V(lambda e: e.tensor_tensor(s_wgb[:], s_B[:], s_dt[:], ALU.mult), inc="vs")
            r.wait("vector", "st", r.mark("st"))
            V(lambda e: e.tensor_copy(s_bf1[:], s_dt[:]), inc="vs")
            r.wait("sync", "vs", r.mark("vs"))
            r.dma("sync", o_dwv[d, 0], s_bf1[:], inc="st")
            V(lambda e, lq=lastq: e.tensor_tensor(
                s_dc[:], s_P[:].rearrange("h (c q) -> h c q", q=Q)[:, :, lq], s_pp[:], ALU.subtract),
              inc="vs")
            r.wait("scalar", "vs", r.mark("vs"))
            A_(lambda e: e.activation(s_dc[:], s_dc[:], AF.Exp), inc="as_")
            r.wait("sync", "as_", r.mark("as_"))
            r.dma("sync", o_dec[d], s_dc[:], inc="st")
            r.dma("sync", o_dectot[d].unsqueeze(1), s_dba[:, 3:4], inc="st")
            V(lambda e: e.tensor_copy(s_plhl[0:32, :], s_PL[:]))
            V(lambda e: e.tensor_tensor(s_plhl[32:64, :], s_PL[:], s_plhl[0:32, :], ALU.subtract), inc="vs")
            r.wait("sync", "vs", r.mark("vs"))
            r.dma("sync", o_pl[d].rearrange("a h t -> (a h) t"), s_plhl[:], inc="st")
            V(lambda e: e.memset(s_dba[:, 3:4], 0.0), inc="vs")
            r.barrier()

            ts0, vs0 = r.mark("ts"), r.mark("vs")
            for c in range(NCH):
                if c >= 1:
                    r.wait("tensor", "vs", vs0 + 4 * c)
                for xc in range(16):
                    r.op("tensor",
                         lambda e, xc=xc, c=c: e.transpose(
                             psb[:, 128 * xc:128 * (xc + 1)],
                             s_xev[:, xc * T + c * Q:xc * T + c * Q + Q], s_ibf[:]),
                         inc=None)
                r.op("tensor",
                     lambda e, c=c: e.transpose(psb[:, 2048:2176],
                                                s_bcst[:, c * Q:c * Q + Q], s_ibf[:]))
                r.op("tensor",
                     lambda e, c=c: e.transpose(psb[:, 2176:2208],
                                                s_wgb[0:32, c * Q:c * Q + Q], s_ibf[0:32, 0:32]),
                     inc="ts")
                r.wait("tensor", "vs", vs0 + 4 * c + 4)
                for h in range(H):
                    r.op("tensor",
                         lambda e, h=h, c=c: e.matmul(
                             ps[0:64, 2048 + 64 * h:2048 + 64 * (h + 1)],
                             s_bct[:, 0:64],
                             s_wgx[:, h * 64:(h + 1) * 64],
                             start=(c == 0 and h % 8 == 0), stop=(c == NCH - 1)),
                         inc="ts" if h == H - 1 else None)
            for c in range(NCH):
                r.wait("vector", "ts", ts0 + 2 * c + 1)
                if c >= 1:
                    r.wait("vector", "st", r.mark("st"))
                r.op("vector", lambda e: e.tensor_copy(s_xtile[:], psb[:, 0:2048]), inc="vs")
                r.op("vector", lambda e: e.tensor_copy(s_bct[:], psb[:, 2048:2176]), inc="vs")
                r.fop("vector", lambda e: e.tensor_copy(s_wgt[:, 0:32], psb[:, 2176:2208]))
                r.op("vector", lambda e: e.nop(), inc="vs")
                r.op("vector",
                     lambda e: e.tensor_tensor(
                         s_wgx[:].rearrange("p (h q) -> p h q", h=H),
                         s_xtile[:].rearrange("p (h q) -> p h q", h=H),
                         s_wgt[:, 0:32].unsqueeze(2).broadcast_to([128, 32, 64]), ALU.mult),
                     inc="vs")
                r.wait("sync", "vs", vs0 + 4 * c + 4)
                r.dma("sync", o_xt[d, c], s_xtile[:], inc="st")
                r.dma("sync", o_bct[d, c], s_bct[:], inc="st")
            r.wait("vector", "ts", r.mark("ts"))
            r.op("vector", lambda e: e.tensor_copy(s_gs[:], ps[0:64, 2048:3072]), inc="vs")
            r.wait("sync", "vs", r.mark("vs"))
            r.dma("sync", o_gsum[d][:, 0:1024], s_gs[:], inc="st")
            r.wait("vector", "st", r.mark("st"))
            r.op("vector", lambda e: e.tensor_copy(s_gs[:], ps[0:64, 3072:4096]), inc="vs")
            r.wait("sync", "vs", r.mark("vs"))
            r.dma("sync", o_gsum[d][:, 1024:2048], s_gs[:], inc="st")
            r.barrier()

        stot = r.mark("st")
        for eng in Rec.ENGS:
            r.wait(eng, "st", stot)
        r.build()
    return nc



def _bf(x):
    return np.ascontiguousarray(x).view(np.uint16) if False else x


def l1_inputs(inputs, core):
    b, q = core // 4, core % 4
    u = np.asarray(inputs["L0_attn_u"], np.float32)
    ush = np.zeros((TS, DM), np.float32)
    t0 = q * T
    lo = max(0, t0 - HOFF)
    ush[HOFF - (t0 - lo):HOFF] = u[b, lo:t0]
    hi = min(8192, t0 + T + 3)
    ush[HOFF:HOFF + T] = u[b, t0:t0 + T]
    ush[HOFF + T:HOFF + T + (hi - t0 - T)] = u[b, t0 + T:hi]

    def wpad(w):
        wt = np.zeros((DM, DPROJ), np.float32)
        wt[:, :4256] = np.ascontiguousarray(np.asarray(w, np.float32).T)
        return wt

    w_inT = np.stack([wpad(inputs["L0_attn_W_in_f"]), wpad(inputs["L0_attn_W_in_b"])])

    def dg(cw):
        cw = np.asarray(cw, np.float32)
        out = np.zeros((CCT, 4, 128, 128), np.float32)
        idx = np.arange(128)
        for ct in range(CCT):
            for k in range(4):
                out[ct, k, idx, idx] = cw[ct * 128:(ct + 1) * 128, k]
        return out

    diag = np.stack([dg(inputs["L0_attn_conv_w_f"]), dg(inputs["L0_attn_conv_w_b"])])
    cb = np.stack([np.asarray(inputs["L0_attn_conv_b_f"], np.float32).reshape(CCT, 128).T,
                   np.asarray(inputs["L0_attn_conv_b_b"], np.float32).reshape(CCT, 128).T])
    dtb_a = np.stack([
        np.stack([np.asarray(inputs["L0_attn_dt_bias_f"], np.float32),
                  np.asarray(inputs["L0_attn_A_log_f"], np.float32)], 1),
        np.stack([np.asarray(inputs["L0_attn_dt_bias_b"], np.float32),
                  np.asarray(inputs["L0_attn_A_log_b"], np.float32)], 1)])
    ibf = np.eye(128, dtype=ml_dtypes.bfloat16)
    return dict(u=ush, w_inT=w_inT, diag=diag, convb=cb, dtb_a=dtb_a, ibf=ibf)





def build_l2a():
    nc = bass.Bass("TRN2", target_bir_lowering=False, debug=False)
    es = contextlib.ExitStack()

    u = nc.dram_tensor("u", [T, DM], F32, kind="ExternalInput").ap()
    wgu = nc.dram_tensor("wgu", [2, I_FFN, DM], F32, kind="ExternalInput").ap()
    clipx = nc.dram_tensor("clipx", [1, 1], F32, kind="ExternalInput").ap()
    maxps = nc.dram_tensor("maxps", [1, 8], F32, kind="ExternalInput").ap()
    ibf = nc.dram_tensor("ibf", [128, 128], BF16, kind="ExternalInput").ap()
    onef = nc.dram_tensor("onef", [1, 128], F32, kind="ExternalInput").ap()

    o_h = nc.dram_tensor("o_h", [ICT, 128, T], F32, kind="ExternalOutput").ap()
    o_maxh = nc.dram_tensor("o_maxh", [1, 1], F32, kind="ExternalOutput").ap()

    a = nc.alloc_sbuf_tensor
    s_ibf = a("s_ibf", [128, 128], BF16)
    s_one = a("s_one", [1, 128], F32)
    s_u = a("s_u", [128, 2 * DM], F32)
    s_c = a("s_c", [128, DM], F32)
    s_st = a("s_st", [128, 8], F32)
    s_xnb = a("s_xnb", [128, DM], BF16)
    s_xqT = a("s_xqT", [128, 8 * T], BF16)
    s_sc1 = a("s_sc1", [1, 16], F32)
    s_scl = a("s_scl", [128, 2], F32)
    s_wf = a("s_wf", [128, 2 * DM], F32)
    s_wq = a("s_wq", [128, DM], BF16)
    s_wT = a("s_wT", [128, 2 * 8 * I_FFN], BF16)
    s_sg = a("s_sg", [128, 2 * 512], BF16)
    s_hst = a("s_hst", [128, 2 * 512], F32)
    s_mx = a("s_mx", [128, 2], F32)
    s_mxf = a("s_mxf", [1, 128], F32)
    s_cst = a("s_cst", [128, 1], F32)

    ps = nc.alloc_psum_tensor("ps", [128, 4096], F32)
    psb = ps[:].bitcast(BF16)

    r = Rec(nc, es, ["ld", "wl", "ts", "vs", "as_", "st", "evs", "evv", "bar", "gp", "fe"])

    with es:
        r.op("vector", lambda e: e.memset(s_cst[:], EPS))
        r.dma("sync", s_ibf[:], ibf)
        r.dma("sync", s_one[:], onef)
        r.dma("sync", s_sc1[:, 0:1], clipx)
        r.dma("sync", s_sc1[:, 1:9], maxps)
        ld0 = r.mark("ld")

        r.wait("vector", "ld", ld0)
        r.fop("vector", lambda e: e.reduce_max(s_sc1[:, 9:10], s_sc1[:, 1:9], AX))
        r.op("vector", lambda e: e.nop(), inc="vs")
        r.wait("scalar", "vs", r.mark("vs"))
        r.op("scalar", lambda e: e.activation(s_sc1[:, 10:11], s_sc1[:, 0:1], AF.Sigmoid), inc="as_")
        r.wait("vector", "as_", r.mark("as_"))
        r.fop("vector", lambda e: e.tensor_tensor(s_sc1[:, 11:12], s_sc1[:, 9:10], s_sc1[:, 10:11], ALU.mult))
        r.fop("vector", lambda e: e.tensor_scalar(s_sc1[:, 13:14], s_sc1[:, 11:12], 1.0 / 127.0, None, ALU.mult))
        r.fop("vector", lambda e: e.reciprocal(s_sc1[:, 12:13], s_sc1[:, 13:14]))
        r.op("vector", lambda e: e.nop(), inc="vs")
        r.wait("tensor", "vs", r.mark("vs"))
        r.op("tensor", lambda e: e.matmul(ps[:, 0:2], s_one[:], s_sc1[:, 12:14], start=True, stop=True),
             inc="ts")
        r.wait("vector", "ts", r.mark("ts"))
        r.fop("vector", lambda e: e.tensor_copy(s_scl[:], ps[:, 0:2]))
        r.op("vector", lambda e: e.nop(), inc="vs")
        r.barrier()

        NTT = 16
        VPT = 7
        vs0, as0, ts0 = r.mark("vs"), r.mark("as_"), r.mark("ts")
        for i in range(NTT):
            if i >= 2:
                r.wait("sync", "vs", vs0 + VPT * (i - 1))
            r.dma("sync", s_u[:, (i % 2) * DM:(i % 2 + 1) * DM], u[128 * i:128 * (i + 1), :])
        ldu0 = ld0
        for i in range(NTT):
            sl = (i % 2) * DM
            r.wait("vector", "ld", ldu0 + 16 * min(i + 2, NTT))
            r.fop("vector", lambda e, sl=sl: e.tensor_reduce(s_st[:, 0:1], s_u[:, sl:sl + DM], AX, ALU.add))
            r.fop("vector", lambda e: e.tensor_scalar(s_st[:, 1:2], s_st[:, 0:1], 1.0 / DM, None, ALU.mult))
            r.op("vector", lambda e, sl=sl: e.tensor_scalar(s_c[:], s_u[:, sl:sl + DM], s_st[:, 1:2], None, ALU.subtract),
                 inc="vs")
            r.wait("scalar", "vs", vs0 + VPT * i + 1)
            r.op("scalar", lambda e: e.activation(s_xnb[:], s_c[:], AF.Square, accum_out=s_st[:, 2:3]),
                 inc="as_")
            r.wait("vector", "as_", as0 + 2 * i + 1)
            r.op("vector", lambda e: e.tensor_scalar(s_st[:, 3:4], s_st[:, 2:3], 1.0 / DM, None, ALU.mult),
                 inc="vs")
            r.wait("scalar", "vs", vs0 + VPT * i + 2)
            r.op("scalar", lambda e: e.activation(s_st[:, 4:5], s_st[:, 3:4], AF.Sqrt, bias=s_cst[:, 0:1]), inc="as_")
            r.wait("vector", "as_", as0 + 2 * i + 2)
            r.fop("vector", lambda e: e.reciprocal(s_st[:, 5:6], s_st[:, 4:5]))
            r.fop("vector", lambda e: e.tensor_tensor(s_st[:, 6:7], s_st[:, 5:6], s_scl[:, 0:1], ALU.mult))
            r.op("vector", lambda e: e.tensor_scalar(s_c[:], s_c[:], s_st[:, 6:7], MAGIC, ALU.mult, ALU.add),
                 inc="vs")
            r.op("vector", lambda e: e.tensor_scalar(s_c[:], s_c[:], MAGIC, 127.0, ALU.subtract, ALU.min),
                 inc="vs")
            r.op("vector", lambda e: e.tensor_scalar(s_xnb[:], s_c[:], -128.0, s_scl[:, 1:2], ALU.max, ALU.mult),
                 inc="vs")
            r.wait("tensor", "vs", vs0 + VPT * i + 5)
            if i >= 1:
                r.wait("tensor", "vs", vs0 + VPT * i)
            for ks in range(8):
                r.op("tensor",
                     lambda e, ks=ks: e.transpose(psb[:, 4096 + 128 * ks:4096 + 128 * (ks + 1)],
                                                  s_xnb[:, 128 * ks:128 * (ks + 1)], s_ibf[:]),
                     inc="ts" if ks == 7 else None)
            r.wait("vector", "ts", ts0 + i + 1)
            r.op("vector",
                 lambda e, i=i: e.tensor_copy(
                     s_xqT[:].rearrange("p (k t) -> p k t", k=8)[:, :, 128 * i:128 * (i + 1)],
                     psb[:, 4096:4096 + 1024].rearrange("p (k t) -> p k t", k=8)),
                 inc="vs")
            r.op("vector", lambda e: e.nop(), inc="vs")
        r.barrier()

        vs0, ts0, wl0 = r.mark("vs"), r.mark("ts"), r.mark("wl")
        WQV = 4
        for wi in range(2):
            for ct in range(ICT):
                j = wi * ICT + ct
                if j >= 2:
                    r.wait("sync", "vs", vs0 + WQV * (j - 1))
                r.dma("sync", s_wf[:, (j % 2) * DM:(j % 2 + 1) * DM], wgu[wi, 128 * ct:128 * (ct + 1), :])
        for wi in range(2):
            for ct in range(ICT):
                j = wi * ICT + ct
                sl = (j % 2) * DM
                r.wait("vector", "ld", r.mark("ld") - 16 * (2 * ICT) + 16 * min(j + 2, 2 * ICT))
                r.fop("vector", lambda e, sl=sl: e.reduce_max(s_st[:, 0:1], s_wf[:, sl:sl + DM], AX,
                                                              apply_absolute_value=True))
                r.fop("vector", lambda e: e.tensor_scalar(s_st[:, 1:2], s_st[:, 0:1], 1.0 / 127.0, None, ALU.mult))
                r.fop("vector", lambda e: e.reciprocal(s_st[:, 2:3], s_st[:, 1:2]))
                r.op("vector", lambda e, sl=sl: e.tensor_scalar(s_c[:], s_wf[:, sl:sl + DM], s_st[:, 2:3], MAGIC, ALU.mult, ALU.add),
                     inc="vs")
                r.op("vector", lambda e: e.tensor_scalar(s_c[:], s_c[:], MAGIC, 127.0, ALU.subtract, ALU.min),
                     inc="vs")
                r.op("vector", lambda e: e.tensor_scalar(s_wq[:], s_c[:], -128.0, s_st[:, 1:2], ALU.max, ALU.mult),
                     inc="vs")
                r.wait("tensor", "vs", vs0 + WQV * j + 3)
                if j >= 1:
                    r.wait("tensor", "vs", vs0 + WQV * j)
                for ks in range(8):
                    r.op("tensor",
                         lambda e, ks=ks: e.transpose(psb[:, 4096 + 128 * ks:4096 + 128 * (ks + 1)],
                                                      s_wq[:, 128 * ks:128 * (ks + 1)], s_ibf[:]),
                         inc="ts" if ks == 7 else None)
                r.wait("vector", "ts", ts0 + j + 1)
                r.op("vector",
                     lambda e, wi=wi, ct=ct: e.tensor_copy(
                         s_wT[:].rearrange("p (w k i) -> p w k i", w=2, k=8)[:, wi, :, 128 * ct:128 * (ct + 1)],
                         psb[:, 4096:4096 + 1024].rearrange("p (k t) -> p k t", k=8)),
                     inc="vs")
        r.barrier()

        ts0, evs0, evv0, st0 = r.mark("ts"), r.mark("evs"), r.mark("evv"), r.mark("st")
        r.op("vector", lambda e: e.memset(s_mx[:], 0.0))
        NTG = 4
        for ct in range(ICT):
            for tg in range(NTG):
                gi = ct * NTG + tg
                if gi >= 2:
                    r.wait("tensor", "evs", evs0 + gi - 1)
                pg = (gi % 2) * 512
                for ks in range(8):
                    r.op("tensor",
                         lambda e, pg=pg, ks=ks, ct=ct, tg=tg:
                         e.matmul(ps[:, pg:pg + 512],
                                  s_wT[:].rearrange("p (w k i) -> p w k i", w=2, k=8)[:, 0, ks, 128 * ct:128 * (ct + 1)],
                                  s_xqT[:].rearrange("p (k t) -> p k t", k=8)[:, ks, 512 * tg:512 * (tg + 1)],
                                  start=(ks == 0), stop=(ks == 7)),
                         inc="ts" if ks == 7 else None)
                if gi >= 2:
                    r.wait("tensor", "evv", evv0 + gi - 1)
                pu = 1024 + (gi % 2) * 512
                for ks in range(8):
                    r.op("tensor",
                         lambda e, pu=pu, ks=ks, ct=ct, tg=tg:
                         e.matmul(ps[:, pu:pu + 512],
                                  s_wT[:].rearrange("p (w k i) -> p w k i", w=2, k=8)[:, 1, ks, 128 * ct:128 * (ct + 1)],
                                  s_xqT[:].rearrange("p (k t) -> p k t", k=8)[:, ks, 512 * tg:512 * (tg + 1)],
                                  start=(ks == 0), stop=(ks == 7)),
                         inc="ts" if ks == 7 else None)
        for ct in range(ICT):
            for tg in range(NTG):
                gi = ct * NTG + tg
                pg = (gi % 2) * 512
                pu = 1024 + (gi % 2) * 512
                r.wait("scalar", "ts", ts0 + 2 * gi + 1)
                r.op("scalar",
                     lambda e, pg=pg, gi=gi: e.activation(s_sg[:, (gi % 2) * 512:(gi % 2) * 512 + 512],
                                                          ps[:, pg:pg + 512], AF.Silu),
                     inc="evs")
                r.wait("vector", "ts", ts0 + 2 * gi + 2)
                r.wait("vector", "evs", evs0 + gi + 1)
                if gi >= 2:
                    r.wait("vector", "st", st0 + 16 * (gi - 1))
                r.op("vector",
                     lambda e, pu=pu, gi=gi: e.tensor_tensor(s_hst[:, (gi % 2) * 512:(gi % 2) * 512 + 512],
                                                             ps[:, pu:pu + 512],
                                                             s_sg[:, (gi % 2) * 512:(gi % 2) * 512 + 512],
                                                             ALU.mult),
                     inc="evv")
                r.fop("vector", lambda e, gi=gi: e.reduce_max(s_st[:, 0:1],
                                                              s_hst[:, (gi % 2) * 512:(gi % 2) * 512 + 512],
                                                              AX, apply_absolute_value=True))
                r.op("vector", lambda e: e.tensor_tensor(s_mx[:, 0:1], s_mx[:, 0:1], s_st[:, 0:1], ALU.max))
                r.wait("sync", "evv", evv0 + gi + 1)
                r.dma("sync", o_h[ct, :, 512 * tg:512 * (tg + 1)],
                      s_hst[:, (gi % 2) * 512:(gi % 2) * 512 + 512], inc="st")
        r.op("vector", lambda e: e.nop(), inc="vs")
        r.wait("gpsimd", "vs", r.mark("vs"))
        r.dma("gpsimd", s_mxf[:], s_mx[:, 0:1], inc="gp")
        r.wait("vector", "gp", r.mark("gp"))
        r.fop("vector", lambda e: e.reduce_max(s_mxf[:, 0:1], s_mxf[:], AX))
        r.op("vector", lambda e: e.nop(), inc="vs")
        r.wait("sync", "vs", r.mark("vs"))
        r.dma("sync", o_maxh, s_mxf[:, 0:1], inc="st")
        stot = r.mark("st")
        for eng in Rec.ENGS:
            r.wait(eng, "st", stot)
        r.build()
    return nc




def build_l2b(dbg=False):
    nc = bass.Bass("TRN2", target_bir_lowering=False, debug=False)
    es = contextlib.ExitStack()

    i_xt = nc.dram_tensor("i_xt", [2, NCH, 128, DIN], BF16, kind="ExternalInput").ap()
    i_sz = nc.dram_tensor("i_sz", [2, NCH, 128, DIN], BF16, kind="ExternalInput").ap()
    i_bc = nc.dram_tensor("i_bc", [2, 128, T], BF16, kind="ExternalInput").ap()
    i_bct = nc.dram_tensor("i_bct", [2, NCH, 128, 128], BF16, kind="ExternalInput").ap()
    i_aux = nc.dram_tensor("i_aux", [2, NCH, 128, 96], BF16, kind="ExternalInput").ap()
    i_plhl = nc.dram_tensor("i_plhl", [2, NCH, 64, 128], BF16, kind="ExternalInput").ap()
    i_plf = nc.dram_tensor("i_plf", [2, NCH, 2, 4096], BF16, kind="ExternalInput").ap()
    i_dcr = nc.dram_tensor("i_dcr", [64, 2, NCH * 32], F32, kind="ExternalInput").ap()
    i_gsum = nc.dram_tensor("i_gsum", [4, 2, 64, DIN], F32, kind="ExternalInput").ap()
    i_dtot = nc.dram_tensor("i_dtot", [1, 256], F32, kind="ExternalInput").ap()
    i_pm = nc.dram_tensor("i_pm", [1, 64], F32, kind="ExternalInput").ap()
    i_di = nc.dram_tensor("i_di", [2, 32, 128, 128], F32, kind="ExternalInput").ap()
    i_wout = nc.dram_tensor("i_wout", [2, DIN, DM], F32, kind="ExternalInput").ap()
    i_nw = nc.dram_tensor("i_nw", [2, 128, 16], F32, kind="ExternalInput").ap()
    i_wfin = nc.dram_tensor("i_wfin", [DIN, DM], F32, kind="ExternalInput").ap()
    i_dexp = nc.dram_tensor("i_dexp", [64, 4096], BF16, kind="ExternalInput").ap()
    i_ut = nc.dram_tensor("i_ut", [2, 128, 128], BF16, kind="ExternalInput").ap()
    i_lt = nc.dram_tensor("i_lt", [2, 128, 128], BF16, kind="ExternalInput").ap()
    i_negi = nc.dram_tensor("i_negi", [128, 128], BF16, kind="ExternalInput").ap()
    i_one2 = nc.dram_tensor("i_one2", [2, 128], BF16, kind="ExternalInput").ap()
    i_onef = nc.dram_tensor("i_onef", [1, 128], F32, kind="ExternalInput").ap()
    i_ones1 = nc.dram_tensor("i_ones1", [128, 1], BF16, kind="ExternalInput").ap()
    ibf = nc.dram_tensor("ibf", [128, 128], BF16, kind="ExternalInput").ap()

    o_s = nc.dram_tensor("o_s", [NCH, 128, DM], BF16, kind="ExternalOutput").ap()
    of_scr = nc.dram_tensor("of_scr", [NCH, 128, DM], BF16, kind="Internal").ap()
    if dbg:
        o_dsm = nc.dram_tensor("o_dsm", [128, 128], F32, kind="ExternalOutput").ap()
        o_dm = nc.dram_tensor("o_dm", [128, 4096], BF16, kind="ExternalOutput").ap()
        o_dsmx = nc.dram_tensor("o_dsmx", [128, 4096], BF16, kind="ExternalOutput").ap()
        o_ddtx = nc.dram_tensor("o_ddtx", [128, DIN], BF16, kind="ExternalOutput").ap()
        o_dwdtx = nc.dram_tensor("o_dwdtx", [128, DIN], BF16, kind="ExternalOutput").ap()
        o_dcv = nc.dram_tensor("o_dcv", [128, DIN], BF16, kind="ExternalOutput").ap()
        o_dyg = nc.dram_tensor("o_dyg", [128, DIN], BF16, kind="ExternalOutput").ap()
        o_dr = nc.dram_tensor("o_dr", [1, 128], F32, kind="ExternalOutput").ap()
        o_dof = nc.dram_tensor("o_dof", [128, DM], BF16, kind="ExternalOutput").ap()
        o_dg0 = nc.dram_tensor("o_dg0", [64, DIN], F32, kind="ExternalOutput").ap()
        o_dg1 = nc.dram_tensor("o_dg1", [64, DIN], F32, kind="ExternalOutput").ap()

    a = nc.alloc_sbuf_tensor
    s_ibf = a("s_ibf", [128, 128], BF16)
    s_ut = a("s_ut", [128, 128], BF16)
    s_lt = a("s_lt", [128, 128], BF16)
    s_negi = a("s_negi", [128, 128], BF16)
    s_ones1 = a("s_ones1", [128, 1], BF16)
    s_onef = a("s_onef", [1, 128], F32)
    s_xt = a("s_xt", [128, 2 * DIN], BF16)
    s_szc = a("s_szc", [128, 2 * DIN], BF16)
    s_bct = a("s_bct", [128, 2 * 128], BF16)
    s_aux = a("s_aux", [128, 2 * 96], BF16)
    s_bn = a("s_bn", [64, T], BF16)
    s_cn = a("s_cn", [64, T], BF16)
    s_dexp = a("s_dexp", [66, 4096], BF16)
    s_dlhs = a("s_dlhs", [66, 2 * 128], BF16)
    s_dcr = a("s_dcr", [64, NCH * 32], F32)
    s_di = a("s_di", [128, 32 * 128], BF16)
    s_wout = a("s_wout", [128, 16 * DM], BF16)
    s_wfin = a("s_wfin", [128, 16 * DM], BF16)
    s_nw = a("s_nw", [128, 16], F32)
    s_sm = a("s_sm", [128, 128], F32)
    s_m = a("s_m", [128, 4096], BF16)
    s_smx = s_m
    s_dtx = a("s_dtx", [128, DIN], BF16)
    s_wdtx = a("s_wdtx", [128, DIN], BF16)
    s_cvt = a("s_cvt", [128, DIN], BF16)
    s_cv = a("s_cv", [64, 2 * DIN], BF16)
    s_yg = a("s_yg", [128, DIN], BF16)
    s_yg2 = s_cvt
    s_g = a("s_g", [64, DIN], F32)
    s_gsc = a("s_gsc", [64, DIN], F32)
    s_gbf = a("s_gbf", [128, DIN], BF16)
    s_gst = a("s_gst", [64, DIN], F32)
    s_wr = a("s_wr", [64, 32], F32)
    s_sc1 = a("s_sc1", [1, 480], F32)
    s_rrow = a("s_rrow", [1, 128], F32)
    s_rcol = a("s_rcol", [128, 1], F32)
    s_ofc = a("s_ofc", [128, 2 * DM], BF16)
    s_ob = a("s_ob", [128, DM], BF16)
    s_sc = s_wdtx
    s_sct = s_dtx
    s_sb = a("s_sb", [128, DM], BF16)

    ps = nc.alloc_psum_tensor("ps", [128, 4096], F32)
    psb = ps[:].bitcast(BF16)
    PS_S = 0
    PS_D = 512
    PS_Y = 1536
    PS_C = 2560
    PS_O = 3584

    r = Rec(nc, es, ["ld", "wl", "ts", "vs", "as_", "st", "gp", "ck", "bar", "fe"])

    def lview(t2d, sl, w):
        return t2d[:, sl * w:(sl + 1) * w]

    with es:
        for dst, srcap in ((s_ibf, ibf), (s_ut, i_ut), (s_negi, i_negi), (s_ones1, i_ones1),
                           (s_onef, i_onef), (s_dcr[:], None)):
            pass
        r.dma("sync", s_ibf[:], ibf)
        r.dma("sync", s_negi[:], i_negi)
        r.dma("sync", s_ones1[:], i_ones1)
        r.dma("sync", s_onef[:], i_onef)
        r.dma("sync", s_dexp[0:64, :], i_dexp)
        r.dma("sync", s_dlhs[64:66, 0:128], i_one2)
        r.dma("sync", s_dlhs[64:66, 128:256], i_one2)
        r.dma("sync", s_sc1[:, 0:256], i_dtot)
        r.dma("sync", s_sc1[:, 256:320], i_pm)
        r.wait("sync", "ld", r.mark("ld"))
        r.barrier()

        for d in range(2):
            r.dma("gpsimd", s_di[:].rearrange("p (h c) -> p h c", h=32), i_di[d].rearrange("h p c -> p h c"))
            r.dma("gpsimd", s_wout[:].rearrange("p (k m) -> p k m", k=16),
                  i_wout[d].rearrange("(k p) m -> p k m", p=128))
            r.dma("sync", s_nw[:], i_nw[d])
            r.dma("sync", s_ut[:], i_ut[d])
            r.dma("sync", s_lt[:], i_lt[d])
            r.dma("sync", s_bn[:], i_bc[d, 0:64, :])
            r.dma("sync", s_cn[:], i_bc[d, 64:128, :])
            r.dma("sync", s_dcr[:], i_dcr[:, d, :])
            if d == 1:
                r.dma("gpsimd", s_wfin[:].rearrange("p (k m) -> p k m", k=16),
                      i_wfin.rearrange("(k p) m -> p k m", p=128))
            r.wait("vector", "ld", r.mark("ld"))
            r.wait("vector", "wl", r.mark("wl"))
            for kt in range(16):
                r.fop("vector", lambda e, kt=kt: e.tensor_scalar(
                    lview(s_wout, kt, DM), lview(s_wout, kt, DM), s_nw[:, kt:kt + 1], None, ALU.mult))
            r.op("vector", lambda e: e.nop(), inc="vs")
            r.barrier()

            for cp in range(4):
                base = 330 + cp * 32
                r.op("vector", lambda e, base=base: e.memset(s_sc1[:, base:base + 32], 1.0))
                for mid in range(4):
                    r.fop("vector", lambda e, mid=mid: e.tensor_scalar(
                        s_sc1[:, 448:480], s_sc1[:, (mid * 2 + d) * 32:(mid * 2 + d) * 32 + 32],
                        1.0, -1.0, ALU.mult, ALU.add))
                    r.fop("vector", lambda e, cp=cp, mid=mid, d=d: e.tensor_scalar(
                        s_sc1[:, 448:480], s_sc1[:, 448:480],
                        s_sc1[:, 256 + d * 16 + cp * 4 + mid:256 + d * 16 + cp * 4 + mid + 1],
                        1.0, ALU.mult, ALU.add))
                    r.fop("vector", lambda e, base=base: e.tensor_tensor(
                        s_sc1[:, base:base + 32], s_sc1[:, base:base + 32], s_sc1[:, 448:480], ALU.mult))
                r.fop("vector", lambda e, base=base, cp=cp, d=d: e.tensor_scalar(
                    s_sc1[:, base:base + 32], s_sc1[:, base:base + 32],
                    s_sc1[:, 288 + d * 4 + cp:288 + d * 4 + cp + 1],
                    None, ALU.mult))
            r.op("vector", lambda e: e.nop(), inc="vs")
            r.op("vector", lambda e: e.memset(s_g[:], 0.0), inc="vs")
            vAdd = 0
            for cp in range(4):
                r.wait("tensor", "vs", r.mark("vs"))
                r.op("tensor", lambda e, cp=cp: e.matmul(ps[0:128, 0:32], s_onef[:],
                                                         s_sc1[:, 330 + cp * 32:330 + cp * 32 + 32],
                                                         start=True, stop=True), inc="ts")
                r.wait("vector", "ts", r.mark("ts"))
                r.fop("vector", lambda e: e.tensor_copy(s_wr[:], ps[0:64, 0:32]))
                if cp >= 1:
                    r.wait("sync", "vs", vAdd)
                r.dma("sync", s_gst[:], i_gsum[cp, d], inc="ld")
                r.wait("vector", "ld", r.mark("ld"))
                r.fop("vector", lambda e: e.tensor_tensor(
                    s_gsc[:].rearrange("n (h p) -> n h p", h=32),
                    s_gst[:].rearrange("n (h p) -> n h p", h=32),
                    s_wr[:].unsqueeze(2).broadcast_to([64, 32, 64]), ALU.mult))
                r.fop("vector", lambda e: e.tensor_tensor(s_g[:], s_g[:], s_gsc[:], ALU.add))
                r.op("vector", lambda e: e.nop(), inc="vs")
                vAdd = r.mark("vs")
            r.fop("vector", lambda e: e.tensor_copy(s_gbf[0:64, :], s_g[:]))
            r.fop("vector", lambda e: e.tensor_copy(s_gbf[64:128, :], s_g[:]))
            r.op("vector", lambda e: e.nop(), inc="vs")
            if dbg and d == 0:
                r.wait("sync", "vs", r.mark("vs"))
                r.dma("sync", o_dg0, s_g[:], inc="st")
            r.barrier()

            chunks = list(range(NCH)) if d == 0 else list(range(NCH - 1, -1, -1))
            rms_done = [0]
            ld0 = r.mark("ld")
            LPC = 6 * 16
            c0 = chunks[0]
            for (dst, srcap) in ((lview(s_xt, 0, DIN), i_xt[d, c0]), (lview(s_szc, 0, DIN), i_sz[d, c0]),
                                 (lview(s_bct, 0, 128), i_bct[d, c0]), (lview(s_aux, 0, 96), i_aux[d, c0]),
                                 (s_dlhs[0:64, 0:128], i_plhl[d, c0]), (s_dexp[64:66, :], i_plf[d, c0])):
                r.dma("sync", dst, srcap, inc="ld")
            for ci, c in enumerate(chunks):
                sl = ci % 2
                if ci + 1 < NCH:
                    cn = chunks[ci + 1]
                    if ci >= 1:
                        r.wait("sync", "ck", r.mark("ck"))
                    nsl = (ci + 1) % 2
                    r.dma("sync", lview(s_xt, nsl, DIN), i_xt[d, cn], inc="ld")
                    r.dma("sync", lview(s_szc, nsl, DIN), i_sz[d, cn], inc="ld")
                    r.dma("sync", lview(s_bct, nsl, 128), i_bct[d, cn], inc="ld")
                    r.dma("sync", lview(s_aux, nsl, 96), i_aux[d, cn], inc="ld")
                    r.dma("sync", s_dlhs[0:64, nsl * 128:nsl * 128 + 128], i_plhl[d, cn], inc="ld")
                cslice = slice(c * Q, (c + 1) * Q)

                r.wait("tensor", "ld", ld0 + LPC * ci + 6 * 16 + (80 if ci + 1 < NCH else 0))
                if ci >= 1:
                    r.wait("tensor", "vs", rms_done[0])
                r.op("tensor", lambda e, cslice=cslice: e.matmul(
                    ps[:, PS_S:PS_S + 128], s_bn[:, cslice], s_cn[:, cslice],
                    start=True, stop=True), inc="ts")
                tS = r.mark("ts")
                r.wait("vector", "ts", tS)
                r.fop("vector", lambda e: e.tensor_tensor(s_sm[:], ps[:, PS_S:PS_S + 128],
                                                          s_ut[:], ALU.mult))
                r.op("vector", lambda e: e.nop(), inc="vs")
                vSm = r.mark("vs")
                if dbg and d == 0 and ci == 0:
                    r.wait("sync", "vs", vSm)
                    r.dma("sync", o_dsm, s_sm[:], inc="st")

                for q4 in range(4):
                    if ci >= 1 or q4 >= 1:
                        r.wait("tensor", "as_", r.mark("as_"))
                    for half in range(2):
                        po = PS_D + half * 512
                        r.op("tensor", lambda e, sl=sl, q4=q4, half=half, po=po: e.matmul(
                            ps[:, po:po + 512],
                            s_dlhs[:, sl * 128:sl * 128 + 128],
                            s_dexp[:, q4 * 1024 + half * 512:q4 * 1024 + half * 512 + 512],
                            start=True, stop=False), inc=None)
                        r.op("tensor", lambda e, q4=q4, half=half, po=po: e.matmul(
                            ps[:, po:po + 512].rearrange("p (h t) -> p h t", h=4),
                            s_negi[:],
                            s_lt[:].unsqueeze(1).broadcast_to([128, 4, 128]),
                            start=False, stop=True), inc="ts" if half == 1 else None)
                    r.wait("scalar", "ts", r.mark("ts"))
                    r.op("scalar", lambda e, q4=q4: e.activation(
                        s_m[:, q4 * 1024:(q4 + 1) * 1024], ps[:, PS_D:PS_D + 1024], AF.Exp),
                        inc="as_")
                if ci + 1 < NCH:
                    r.wait("sync", "ts", r.mark("ts"))
                    r.dma("sync", s_dexp[64:66, :], i_plf[d, chunks[ci + 1]], inc="ld")

                r.wait("vector", "as_", r.mark("as_"))
                if dbg and d == 0 and ci == 0:
                    r.wait("sync", "as_", r.mark("as_"))
                    r.dma("sync", o_dm, s_m[:], inc="st")
                    r.wait("vector", "st", r.mark("st"))
                r.op("vector", lambda e: e.tensor_tensor(
                    s_smx[:].rearrange("p (h t) -> p h t", h=32),
                    s_m[:].rearrange("p (h t) -> p h t", h=32),
                    s_sm[:].unsqueeze(1).broadcast_to([128, 32, 128]), ALU.mult), inc="vs")
                r.op("vector", lambda e, sl=sl: e.tensor_tensor(
                    s_dtx[:].rearrange("p (h q) -> p h q", h=32),
                    lview(s_xt, sl, DIN).rearrange("p (h q) -> p h q", h=32),
                    s_aux[:, sl * 96:sl * 96 + 32].unsqueeze(2).broadcast_to([128, 32, 64]),
                    ALU.mult), inc="vs")
                r.op("vector", lambda e, sl=sl: e.tensor_tensor(
                    s_wdtx[:].rearrange("p (h q) -> p h q", h=32),
                    lview(s_xt, sl, DIN).rearrange("p (h q) -> p h q", h=32),
                    s_aux[:, sl * 96 + 32:sl * 96 + 64].unsqueeze(2).broadcast_to([128, 32, 64]),
                    ALU.mult), inc="vs")
                r.op("vector", lambda e, sl=sl: e.tensor_tensor(
                    s_cvt[:].rearrange("p (h n) -> p h n", h=32),
                    s_bct[:, sl * 128 + 64:sl * 128 + 128].unsqueeze(1).broadcast_to([128, 32, 64]),
                    s_aux[:, sl * 96 + 64:sl * 96 + 96].unsqueeze(2).broadcast_to([128, 32, 64]),
                    ALU.mult), inc="vs")
                vPrep = r.mark("vs")
                if dbg and d == 0 and ci == 0:
                    r.wait("sync", "vs", vPrep)
                    r.dma("sync", o_dsmx, s_smx[:], inc="st")
                    r.dma("sync", o_ddtx, s_dtx[:], inc="st")
                    r.dma("sync", o_dwdtx, s_wdtx[:], inc="st")

                r.wait("tensor", "vs", vPrep)
                r.wait("tensor", "as_", r.mark("as_"))
                for rnd in range(2):
                    for hh in range(16):
                        h = rnd * 16 + hh
                        r.op("tensor", lambda e, h=h, hh=hh: e.transpose(
                            psb[0:64, 1024 + 128 * hh:1024 + 128 * (hh + 1)],
                            s_cvt[:, 64 * h:64 * (h + 1)], s_ibf[:]),
                            inc="ts" if hh == 15 else None)
                    r.wait("vector", "ts", r.mark("ts"))
                    r.fop("vector", lambda e, rnd=rnd: e.tensor_copy(
                        s_cv[:, 2048 * rnd:2048 * (rnd + 1)], psb[0:64, 1024:1024 + 2048]))
                    if rnd == 0:
                        r.op("vector", lambda e: e.nop(), inc="vs")
                        r.wait("tensor", "vs", r.mark("vs"))
                r.op("vector", lambda e: e.nop(), inc="vs")
                vCv = r.mark("vs")
                if dbg and d == 0 and ci == 0:
                    r.wait("sync", "vs", vCv)
                    r.dma("sync", o_dcv, s_cv[:], inc="st")

                ygate = vCv
                for grp in range(2):
                    r.wait("tensor", "vs", ygate)
                    for hh in range(16):
                        h = grp * 16 + hh
                        po = PS_Y + 128 * hh
                        r.op("tensor", lambda e, h=h, po=po: e.matmul(
                            ps[0:64, po:po + 128],
                            s_dtx[:, 64 * h:64 * (h + 1)],
                            s_smx[:, 128 * h:128 * (h + 1)],
                            start=True, stop=False), inc=None)
                        r.op("tensor", lambda e, h=h, po=po: e.matmul(
                            ps[0:64, po:po + 128],
                            s_gbf[0:64, 64 * h:64 * (h + 1)],
                            s_cv[:, 2048 * (h // 16) + 128 * (h % 16):2048 * (h // 16) + 128 * (h % 16) + 128],
                            start=False, stop=False), inc=None)
                        r.op("tensor", lambda e, h=h, po=po, sl=sl: e.matmul(
                            ps[0:64, po:po + 128],
                            s_xt[:, sl * DIN + 64 * h:sl * DIN + 64 * (h + 1)],
                            s_di[:, 128 * h:128 * (h + 1)],
                            start=False, stop=True), inc="ts" if hh == 15 else None)
                    r.wait("vector", "ts", r.mark("ts"))
                    for par in range(2):
                        r.op("vector", lambda e, grp=grp, par=par, sl=sl: e.tensor_tensor(
                            s_yg[64 * par:64 * (par + 1), 1024 * grp:1024 * (grp + 1)]
                                .rearrange("p (ct t) -> p ct t", ct=8),
                            ps[0:64, PS_Y:PS_Y + 2048]
                                .rearrange("p (h t) -> p h t", h=16)[:, par::2, :],
                            s_szc[64 * par:64 * (par + 1),
                                  sl * DIN + 1024 * grp:sl * DIN + 1024 * (grp + 1)]
                                .rearrange("p (ct t) -> p ct t", ct=8),
                            ALU.mult), inc="vs")
                    ygate = r.mark("vs")
                    r.wait("scalar", "vs", ygate)
                    r.op("scalar", lambda e, grp=grp: e.activation(
                        s_yg2[:, 1024 * grp:1024 * (grp + 1)], s_yg[:, 1024 * grp:1024 * (grp + 1)],
                        AF.Square), inc="as_")

                if dbg and d == 0 and ci == 0:
                    r.wait("sync", "vs", r.mark("vs"))
                    r.dma("sync", o_dyg, s_yg[:], inc="st")
                r.wait("tensor", "as_", r.mark("as_"))
                for kt in range(16):
                    r.op("tensor", lambda e, kt=kt: e.matmul(
                        ps[0:1, PS_S + 128:PS_S + 256],
                        s_ones1[:], s_yg2[:, 128 * kt:128 * (kt + 1)],
                        start=(kt == 0), stop=(kt == 15)), inc="ts" if kt == 15 else None)
                r.wait("vector", "ts", r.mark("ts"))
                r.fop("vector", lambda e: e.tensor_scalar(s_rrow[:], ps[0:1, PS_S + 128:PS_S + 256],
                                                          1.0 / DIN, EPS, ALU.mult, ALU.add))
                r.op("vector", lambda e: e.nop(), inc="vs")
                rms_ev = r.mark("vs")
                rms_done[0] = rms_ev
                r.wait("scalar", "vs", rms_ev)
                r.op("scalar", lambda e: e.activation(s_rrow[:], s_rrow[:], AF.Ln), inc="as_")
                r.op("scalar", lambda e: e.activation(s_rrow[:], s_rrow[:], AF.Exp, scale=-0.5), inc="as_")
                r.wait("gpsimd", "as_", r.mark("as_"))
                if dbg and d == 0 and ci == 0:
                    r.wait("sync", "as_", r.mark("as_"))
                    r.dma("sync", o_dr, s_rrow[:], inc="st")
                r.dma("gpsimd", s_rcol[:, 0:1], s_rrow[:], inc="gp")

                for dmg in range(2):
                    if dmg == 1:
                        r.wait("tensor", "vs", r.mark("vs"))
                    for kt in range(16):
                        r.op("tensor", lambda e, kt=kt, dmg=dmg: e.matmul(
                            ps[:, PS_O:PS_O + 512],
                            s_yg[:, 128 * kt:128 * (kt + 1)],
                            s_wout[:, kt * DM + dmg * 512:kt * DM + dmg * 512 + 512],
                            start=(kt == 0), stop=(kt == 15)), inc="ts" if kt == 15 else None)
                    r.wait("vector", "ts", r.mark("ts"))
                    if dmg == 0:
                        r.wait("vector", "gp", r.mark("gp"))
                    dst = (lambda e, dmg=dmg: e.tensor_scalar(
                        s_ofc[:, dmg * 512:dmg * 512 + 512],
                        ps[:, PS_O:PS_O + 512], s_rcol[:, 0:1], None, ALU.mult)) if d == 0 else \
                          (lambda e, dmg=dmg: e.tensor_scalar(
                        s_ob[:, dmg * 512:dmg * 512 + 512],
                        ps[:, PS_O:PS_O + 512], s_rcol[:, 0:1], None, ALU.mult))
                    r.op("vector", dst, inc="vs")
                vOf = r.mark("vs")
                if d == 0:
                    r.wait("sync", "vs", vOf)
                    r.dma("sync", of_scr[c], s_ofc[:, 0:DM], inc="st")
                    r.wait("vector", "st", r.mark("st"))
                if dbg and d == 0 and ci == 0:
                    r.wait("sync", "vs", vOf)
                    r.dma("sync", o_dof, s_ofc[:, 0:DM], inc="st")


                for grp in range(2):
                    if grp == 1:
                        r.wait("tensor", "vs", r.mark("vs"))
                    for hh in range(16):
                        h = grp * 16 + hh
                        r.op("tensor", lambda e, h=h, hh=hh, sl=sl, grp=grp: e.matmul(
                            ps[0:64, PS_Y + 64 * hh:PS_Y + 64 * (hh + 1)],
                            s_bct[:, sl * 128:sl * 128 + 64],
                            s_wdtx[:, 64 * h:64 * (h + 1)],
                            start=(hh % 8 == 0), stop=(hh % 8 == 7)),
                            inc="ts" if hh == 15 else None)
                    r.wait("vector", "ts", r.mark("ts"))
                    r.op("vector", lambda e, grp=grp, c=c: e.tensor_tensor(
                        s_gsc[:, 1024 * grp:1024 * (grp + 1)].rearrange("n (h p) -> n h p", h=16),
                        s_g[:, 1024 * grp:1024 * (grp + 1)].rearrange("n (h p) -> n h p", h=16),
                        s_dcr[:, c * 32 + grp * 16:c * 32 + grp * 16 + 16]
                            .unsqueeze(2).broadcast_to([64, 16, 64]), ALU.mult), inc="vs")
                    r.fop("vector", lambda e, grp=grp: e.tensor_tensor(
                        s_g[:, 1024 * grp:1024 * (grp + 1)],
                        s_gsc[:, 1024 * grp:1024 * (grp + 1)],
                        ps[0:64, PS_Y:PS_Y + 1024], ALU.add))
                    r.op("vector", lambda e: e.nop(), inc="vs")
                r.fop("vector", lambda e: e.tensor_copy(s_gbf[0:64, :], s_g[:]))
                r.fop("vector", lambda e: e.tensor_copy(s_gbf[64:128, :], s_g[:]))
                if dbg and d == 0 and ci == 0:
                    r.wait("sync", "fe", r.mark("fe"))
                    r.dma("sync", o_dg1, s_g[:], inc="st")
                if d == 1:
                    r.dma("sync", s_ofc[:, 0:DM], of_scr[c], inc="wl")
                    r.wait("scalar", "wl", r.mark("wl"))
                    r.wait("scalar", "vs", vOf)
                    r.op("scalar", lambda e: e.activation(
                        s_sc[:, 0:1024], s_ofc[:, 0:DM], AF.Silu), inc="as_")
                    r.op("scalar", lambda e: e.activation(
                        s_sc[:, 1024:2048], s_ob[:], AF.Silu), inc="as_")
                    r.wait("tensor", "as_", r.mark("as_"))
                    r.wait("tensor", "vs", r.mark("vs"))
                    for j in range(16):
                        r.op("tensor", lambda e, j=j: e.transpose(
                            psb[:, 1024 + 128 * j:1024 + 128 * (j + 1)],
                            s_sc[:, 128 * j:128 * (j + 1)], s_ibf[:]),
                            inc="ts" if j == 15 else None)
                    r.wait("vector", "ts", r.mark("ts"))
                    r.fop("vector", lambda e: e.tensor_copy(s_sct[:], psb[:, 1024:1024 + 2048]))
                    r.op("vector", lambda e: e.nop(), inc="vs")
                    r.wait("tensor", "vs", r.mark("vs"))
                    for dmg in range(2):
                        if dmg == 1:
                            r.wait("tensor", "vs", r.mark("vs"))
                        for j in range(16):
                            r.op("tensor", lambda e, j=j, dmg=dmg: e.matmul(
                                ps[:, PS_O:PS_O + 512],
                                s_sct[:, 128 * j:128 * (j + 1)],
                                s_wfin[:, j * DM + dmg * 512:j * DM + dmg * 512 + 512],
                                start=(j == 0), stop=(j == 15)), inc="ts" if j == 15 else None)
                        r.wait("vector", "ts", r.mark("ts"))
                        if ci >= 1 and dmg == 0:
                            r.wait("vector", "st", r.mark("st"))
                        r.op("vector", lambda e, dmg=dmg: e.tensor_copy(
                            s_sb[:, dmg * 512:dmg * 512 + 512], ps[:, PS_O:PS_O + 512]), inc="vs")
                    r.wait("sync", "vs", r.mark("vs"))
                    r.dma("sync", o_s[c], s_sb[:], inc="st")
                r.op("vector", lambda e: e.nop(), inc="ck")
            r.barrier()

        stot = r.mark("st")
        for eng in Rec.ENGS:
            r.wait(eng, "st", stot)
        r.build()
    return nc




def build_l3(dbg=False, paranoid=0):
    nc = bass.Bass("TRN2", target_bir_lowering=False, debug=False)
    es = contextlib.ExitStack()

    u = nc.dram_tensor("u", [T, DM], F32, kind="ExternalInput").ap()
    i_s = nc.dram_tensor("i_s", [NCH, 128, DM], BF16, kind="ExternalInput").ap()
    i_h = nc.dram_tensor("i_h", [ICT, 128, T], F32, kind="ExternalInput").ap()
    wd = nc.dram_tensor("wd", [DM, I_FFN], F32, kind="ExternalInput").ap()
    clipd = nc.dram_tensor("clipd", [1, 1], F32, kind="ExternalInput").ap()
    maxhs = nc.dram_tensor("maxhs", [1, 8], F32, kind="ExternalInput").ap()
    ibf = nc.dram_tensor("ibf", [128, 128], BF16, kind="ExternalInput").ap()
    onef = nc.dram_tensor("onef", [1, 128], F32, kind="ExternalInput").ap()

    o_out = nc.dram_tensor("o_out", [T, DM], F32, kind="ExternalOutput").ap()
    if dbg:
        o_dhq = nc.dram_tensor("o_dhq", [128, ICT * 512], BF16, kind="ExternalOutput").ap()
        o_dwq = nc.dram_tensor("o_dwq", [128, ICT * 128], BF16, kind="ExternalOutput").ap()
        o_dg = nc.dram_tensor("o_dg", [128, DM], BF16, kind="ExternalOutput").ap()
        o_dscl = nc.dram_tensor("o_dscl", [128, 2], F32, kind="ExternalOutput").ap()

    a = nc.alloc_sbuf_tensor
    s_ibf = a("s_ibf", [128, 128], BF16)
    s_one = a("s_one", [1, 128], F32)
    s_sc1 = a("s_sc1", [1, 16], F32)
    s_scl = a("s_scl", [128, 2], F32)
    s_wf = a("s_wf", [128, 2 * I_FFN], F32)
    s_wq = a("s_wq", [128, I_FFN], BF16)
    s_wT = a("s_wT", [128, ICT * DM], BF16)
    s_hf = a("s_hf", [128, 2 * 512], F32)
    s_hq = a("s_hq", [128, ICT * 512], BF16)
    s_st = a("s_st", [128, 8], F32)
    s_u = a("s_u", [128, 2 * DM], F32)
    s_sb = a("s_sb", [128, 2 * DM], BF16)
    s_gb = a("s_gb", [128, 2 * DM], BF16)
    s_o = a("s_o", [128, 2 * DM], F32)

    ps = nc.alloc_psum_tensor("ps", [128, 4096], F32)
    psb = ps[:].bitcast(BF16)

    r = Rec(nc, es, ["ld", "wl", "ts", "vs", "as_", "st", "evs", "bar", "gp", "fe"])

    with es:
        r.dma("sync", s_ibf[:], ibf)
        r.dma("sync", s_one[:], onef)
        r.dma("sync", s_sc1[:, 0:1], clipd)
        r.dma("sync", s_sc1[:, 1:9], maxhs)
        ld0 = r.mark("ld")
        r.wait("vector", "ld", ld0)
        r.fop("vector", lambda e: e.reduce_max(s_sc1[:, 9:10], s_sc1[:, 1:9], AX))
        r.op("vector", lambda e: e.nop(), inc="vs")
        r.wait("scalar", "vs", r.mark("vs"))
        r.op("scalar", lambda e: e.activation(s_sc1[:, 10:11], s_sc1[:, 0:1], AF.Sigmoid), inc="as_")
        r.wait("vector", "as_", r.mark("as_"))
        r.fop("vector", lambda e: e.tensor_tensor(s_sc1[:, 11:12], s_sc1[:, 9:10], s_sc1[:, 10:11], ALU.mult))
        r.fop("vector", lambda e: e.tensor_scalar(s_sc1[:, 13:14], s_sc1[:, 11:12], 1.0 / 127.0, None, ALU.mult))
        r.fop("vector", lambda e: e.reciprocal(s_sc1[:, 12:13], s_sc1[:, 13:14]))
        r.op("vector", lambda e: e.nop(), inc="vs")
        r.wait("tensor", "vs", r.mark("vs"))
        r.op("tensor", lambda e: e.matmul(ps[:, 0:2], s_one[:], s_sc1[:, 12:14], start=True, stop=True),
             inc="ts")
        r.wait("vector", "ts", r.mark("ts"))
        r.fop("vector", lambda e: e.tensor_copy(s_scl[:], ps[:, 0:2]))
        r.op("vector", lambda e: e.nop(), inc="vs")
        r.barrier()

        vs0, ts0 = r.mark("vs"), r.mark("ts")
        WQV = 4
        for j in range(8):
            if j >= 2:
                r.wait("sync", "vs", vs0 + WQV * (j - 1))
            r.dma("sync", s_wf[:, (j % 2) * I_FFN:(j % 2 + 1) * I_FFN], wd[128 * j:128 * (j + 1), :])
        ldw0 = ld0
        for j in range(8):
            sl = (j % 2) * I_FFN
            r.wait("vector", "ld", ldw0 + 16 * min(j + 2, 8))
            r.fop("vector", lambda e, sl=sl: e.reduce_max(s_st[:, 0:1], s_wf[:, sl:sl + I_FFN], AX,
                                                          apply_absolute_value=True))
            r.fop("vector", lambda e: e.tensor_scalar(s_st[:, 1:2], s_st[:, 0:1], 1.0 / 127.0, None, ALU.mult))
            r.fop("vector", lambda e: e.reciprocal(s_st[:, 2:3], s_st[:, 1:2]))
            r.op("vector", lambda e, sl=sl: e.tensor_scalar(s_wf[:, sl:sl + I_FFN], s_wf[:, sl:sl + I_FFN], s_st[:, 2:3], MAGIC, ALU.mult, ALU.add),
                 inc="vs")
            r.op("vector", lambda e, sl=sl: e.tensor_scalar(s_wf[:, sl:sl + I_FFN], s_wf[:, sl:sl + I_FFN], MAGIC, 127.0, ALU.subtract, ALU.min),
                 inc="vs")
            r.op("vector", lambda e, sl=sl: e.tensor_scalar(s_wq[:], s_wf[:, sl:sl + I_FFN], -128.0, s_st[:, 1:2], ALU.max, ALU.mult),
                 inc="vs")
            r.wait("tensor", "vs", vs0 + WQV * j + 3)
            if j >= 1:
                r.wait("tensor", "vs", vs0 + WQV * j)
            for it in range(ICT):
                r.op("tensor",
                     lambda e, it=it: e.transpose(psb[:, 4096 + 128 * it:4096 + 128 * (it + 1)],
                                                  s_wq[:, 128 * it:128 * (it + 1)], s_ibf[:]),
                     inc="ts" if it == ICT - 1 else None)
            r.wait("vector", "ts", ts0 + j + 1)
            r.op("vector",
                 lambda e, j=j: e.tensor_copy(
                     s_wT[:].rearrange("p (it dm) -> p it dm", it=ICT)[:, :, 128 * j:128 * (j + 1)],
                     psb[:, 4096:4096 + ICT * 128].rearrange("p (it t) -> p it t", it=ICT)),
                 inc="vs")
        r.barrier()

        ts0, vs0, st0 = r.mark("ts"), r.mark("vs"), r.mark("st")
        ldh0 = r.mark("ld")
        VPB = ICT * 3 + 8
        for tb in range(4):
            for it in range(ICT):
                if paranoid >= 3 and (it >= 1 or tb >= 1):
                    r.wait("sync", "vs", vs0 + VPB * tb + 3 * it if it >= 1
                           else vs0 + VPB * (tb - 1) + 3 * ICT)
                elif it >= 2:
                    r.wait("sync", "vs", vs0 + VPB * tb + 3 * (it - 1))
                elif tb >= 1:
                    r.wait("sync", "vs", vs0 + VPB * (tb - 1) + 3 * ICT)
                r.dma("sync", s_hf[:, (it % 2) * 512:(it % 2 + 1) * 512],
                      i_h[it, :, 512 * tb:512 * (tb + 1)], inc="ld")
            r.dma("sync", s_u[:, 0:DM], u[512 * tb:512 * tb + 128, :], inc="ld")
            r.dma("sync", s_u[:, DM:2 * DM], u[512 * tb + 128:512 * tb + 256, :], inc="ld")
            r.dma("sync", s_sb[:, 0:DM], i_s[4 * tb], inc="ld")
            r.dma("sync", s_sb[:, DM:2 * DM], i_s[4 * tb + 1], inc="ld")
            base = r.mark("ld") - 16 * (ICT + 4)
            for it in range(ICT):
                r.wait("vector", "ld", base + 16 * (it + 2) if it <= ICT - 3
                       else base + 16 * (ICT + 4))
                sl = (it % 2) * 512
                r.fop("vector", lambda e, sl=sl: e.tensor_scalar(s_hf[:, sl:sl + 512], s_hf[:, sl:sl + 512], s_scl[:, 0:1], MAGIC, ALU.mult, ALU.add),
                      inc="vs")
                r.fop("vector", lambda e, sl=sl: e.tensor_scalar(s_hf[:, sl:sl + 512], s_hf[:, sl:sl + 512], MAGIC, 127.0, ALU.subtract, ALU.min),
                      inc="vs")
                r.fop("vector", lambda e, sl=sl, it=it: e.tensor_scalar(s_hq[:, it * 512:(it + 1) * 512], s_hf[:, sl:sl + 512], -128.0, s_scl[:, 1:2], ALU.max, ALU.mult),
                      inc="vs")
            r.wait("tensor", "vs", r.mark("vs"))
            for ttl in range(4):
                tt = 4 * tb + ttl
                if tb + ttl > 0:
                    r.wait("tensor", "as_", r.mark("as_"))
                for dmg in range(2):
                    po = (dmg % 2) * 512
                    for it in range(ICT):
                        r.op("tensor", lambda e, it=it, ttl=ttl, dmg=dmg, po=po: e.matmul(
                            ps[:, po:po + 512],
                            s_hq[:, it * 512 + 128 * ttl:it * 512 + 128 * (ttl + 1)],
                            s_wT[:, it * DM + dmg * 512:it * DM + dmg * 512 + 512],
                            start=(it == 0), stop=(it == ICT - 1)),
                            inc="ts" if it == ICT - 1 else None)
                r.wait("scalar", "ts", r.mark("ts"))
                g = 4 * tb + ttl
                gh = (g % 2) * DM
                if g >= 2:
                    tb2, ttl2 = (g - 2) // 4, (g - 2) % 4
                    r.wait("scalar", "vs", vs0 + VPB * tb2 + 3 * ICT + 2 * (ttl2 + 1))
                r.op("scalar", lambda e, gh=gh: e.activation(s_gb[:, gh:gh + 512], ps[:, 0:512], AF.Silu), inc="as_")
                r.op("scalar", lambda e, gh=gh: e.activation(s_gb[:, gh + 512:gh + 1024], ps[:, 512:1024], AF.Silu), inc="as_")
                if ttl == 2:
                    r.wait("sync", "st", r.mark("st"))
                    r.dma("sync", s_u[:, 0:DM], u[512 * tb + 256:512 * tb + 384, :], inc="ld")
                    r.dma("sync", s_u[:, DM:2 * DM], u[512 * tb + 384:512 * tb + 512, :], inc="ld")
                    r.dma("sync", s_sb[:, 0:DM], i_s[4 * tb + 2], inc="ld")
                    r.dma("sync", s_sb[:, DM:2 * DM], i_s[4 * tb + 3], inc="ld")
                r.wait("vector", "ld", r.mark("ld"))
                r.wait("vector", "as_", r.mark("as_"))
                half = (ttl % 2) * DM
                if g >= 2:
                    r.wait("vector", "st", st0 + 16 * g)
                r.op("vector", lambda e, half=half: e.tensor_tensor(
                    s_o[:, half:half + DM],
                    s_sb[:, half:half + DM], s_gb[:, half:half + DM], ALU.mult), inc="vs")
                r.op("vector", lambda e, half=half: e.tensor_tensor(
                    s_o[:, half:half + DM], s_o[:, half:half + DM], s_u[:, half:half + DM], ALU.add), inc="vs")
                r.wait("sync", "vs", r.mark("vs"))
                r.dma("sync", o_out[128 * tt:128 * (tt + 1), :], s_o[:, half:half + DM], inc="st")
                if paranoid >= 2:
                    for eng in Rec.ENGS:
                        r.wait(eng, "st", r.mark("st"))
                    r.barrier()
            if paranoid == 1:
                for eng in Rec.ENGS:
                    r.wait(eng, "st", r.mark("st"))
                r.barrier()
        if dbg:
            r.barrier()
            r.dma("sync", o_dg, s_gb[:, DM:2 * DM], inc="st")
            r.dma("sync", o_dwq, s_wT[:, 0:ICT * 128], inc="st")
            r.dma("sync", o_dhq, s_hq[:], inc="st")
            r.dma("sync", o_dscl, s_scl[:], inc="st")
        stot = r.mark("st")
        for eng in Rec.ENGS:
            r.wait(eng, "st", stot)
        r.build()
    return nc



def _f32(x):
    return np.ascontiguousarray(np.asarray(x, np.float32))


def l2b_inputs(inputs, res1):
    ibf = np.eye(128, dtype=ml_dtypes.bfloat16)
    onef = np.ones((1, 128), np.float32)
    dexp = np.zeros((64, 4096), np.float32)
    for h in range(32):
        dexp[h, h * 128:(h + 1) * 128] = -1.0
        dexp[32 + h, h * 128:(h + 1) * 128] = -1.0
    dexp = dexp.astype(ml_dtypes.bfloat16)
    tu = np.triu(np.ones((128, 128), np.float32))
    tl = np.tril(np.ones((128, 128), np.float32))
    utm = np.stack([tu, tl]).astype(ml_dtypes.bfloat16)
    ltm = np.stack([1.0 - tu, 1.0 - tl]).astype(ml_dtypes.bfloat16)
    negi = (np.eye(128, dtype=np.float32) * NEGM).astype(ml_dtypes.bfloat16)
    one2 = np.ones((2, 128), np.float32).astype(ml_dtypes.bfloat16)
    ones1 = np.ones((128, 1), np.float32).astype(ml_dtypes.bfloat16)
    woutT = np.stack([_f32(inputs["L0_attn_W_out_f"]).T, _f32(inputs["L0_attn_W_out_b"]).T])
    nw = np.stack([_f32(inputs["L0_attn_norm_w_f"]).reshape(16, 128).T,
                   _f32(inputs["L0_attn_norm_w_b"]).reshape(16, 128).T])
    wfinT = _f32(inputs["L0_attn_W_out"]).T
    DI = np.zeros((2, 32, 128, 128), np.float32)
    idx = np.arange(128)
    for d, sfx in ((0, "f"), (1, "b")):
        Dv = _f32(inputs[f"L0_attn_D_{sfx}"])
        for h in range(32):
            DI[d, h, idx, idx] = Dv[h]
    ims2b = []
    for c in range(8):
        b, q = c // 4, c % 4
        r1 = res1[c]
        aux = np.asarray(r1["o_dwv"])
        auxT = np.ascontiguousarray(aux.transpose(0, 3, 1, 2))
        auxT = auxT.reshape(2, NCH, 128, 3 * 32)
        pl = np.asarray(r1["o_pl"])
        plhl = np.ascontiguousarray(
            pl.reshape(2, 2, 32, NCH, 128).transpose(0, 3, 1, 2, 4).reshape(2, NCH, 64, 128))
        plf = np.ascontiguousarray(
            pl.reshape(2, 2, 32, NCH, 128).transpose(0, 3, 1, 2, 4).reshape(2, NCH, 2, 32 * 128))
        dec = np.asarray(r1["o_dec"], np.float32)
        dcr = np.ascontiguousarray(
            np.broadcast_to(dec.transpose(0, 2, 1).reshape(1, 2, NCH * 32), (64, 2, NCH * 32)))
        sz = np.asarray(r1["o_sz"]).reshape(2, 16, 128, NCH, 128)
        szc = np.ascontiguousarray(sz.transpose(0, 3, 2, 1, 4).reshape(2, NCH, 128, DIN))
        grp = [b * 4 + j for j in range(4)]
        gsum = np.stack([np.asarray(res1[g]["o_gsum"], np.float32) for g in grp])
        dtot = np.zeros((1, 256), np.float32)
        for j, g in enumerate(grp):
            dt_ = np.asarray(res1[g]["o_dectot"], np.float32)
            for d in range(2):
                dtot[0, (j * 2 + d) * 32:(j * 2 + d) * 32 + 32] = dt_[d]
        pm = np.zeros((1, 64), np.float32)
        for cp in range(4):
            for mid in range(4):
                pm[0, cp * 4 + mid] = 1.0 if (cp < mid < q) else 0.0
                pm[0, 16 + cp * 4 + mid] = 1.0 if (q < mid < cp) else 0.0
            pm[0, 32 + cp] = 1.0 if cp < q else 0.0
            pm[0, 36 + cp] = 1.0 if cp > q else 0.0
        ims2b.append(dict(
            i_xt=np.asarray(r1["o_xt"]), i_sz=szc.astype(ml_dtypes.bfloat16) if szc.dtype != ml_dtypes.bfloat16 else szc,
            i_bc=np.asarray(r1["o_bc"]),
            i_bct=np.asarray(r1["o_bct"]), i_aux=auxT, i_plhl=plhl, i_plf=plf,
            i_dcr=dcr, i_gsum=gsum, i_dtot=dtot, i_pm=pm, i_di=DI,
            i_wout=woutT, i_nw=nw, i_wfin=wfinT, i_dexp=dexp, i_ut=utm, i_lt=ltm,
            i_negi=negi, i_one2=one2, i_onef=onef, i_ones1=ones1, ibf=ibf))

    return ims2b


def kernel(**inputs):
    B = 2
    u_full = _f32(inputs["L0_attn_u"])

    nc1 = build_l1()
    ims1 = [l1_inputs(inputs, c) for c in range(8)]
    res1 = run_bass_kernel_spmd(nc1, ims1, core_ids=list(range(8))).results

    nc2a = build_l2a()
    wgu = np.stack([_f32(inputs["L0_ffn_w_gate"]), _f32(inputs["L0_ffn_w_up"])])
    maxps = np.concatenate([np.asarray(res1[c]["o_maxp"], np.float32).reshape(1, 1) for c in range(8)], 1)
    ibf = np.eye(128, dtype=ml_dtypes.bfloat16)
    onef = np.ones((1, 128), np.float32)
    ims2a = []
    for c in range(8):
        b, q = c // 4, c % 4
        ims2a.append(dict(
            u=u_full[b, q * T:(q + 1) * T], wgu=wgu,
            clipx=_f32(inputs["L0_ffn_clip_x"]).reshape(1, 1),
            maxps=maxps, ibf=ibf, onef=onef))
    res2a = run_bass_kernel_spmd(nc2a, ims2a, core_ids=list(range(8))).results

    nc2b = build_l2b()
    ims2b = l2b_inputs(inputs, res1)
    res2b = run_bass_kernel_spmd(nc2b, ims2b, core_ids=list(range(8))).results

    nc3 = build_l3()
    maxhs = np.concatenate([np.asarray(res2a[c]["o_maxh"], np.float32).reshape(1, 1) for c in range(8)], 1)
    ims3 = []
    for c in range(8):
        b, q = c // 4, c % 4
        ims3.append(dict(
            u=u_full[b, q * T:(q + 1) * T],
            i_s=np.asarray(res2b[c]["o_s"]),
            i_h=np.asarray(res2a[c]["o_h"]),
            wd=_f32(inputs["L0_ffn_w_down"]),
            clipd=_f32(inputs["L0_ffn_clip_down"]).reshape(1, 1),
            maxhs=maxhs, ibf=ibf, onef=onef))
    res3 = run_bass_kernel_spmd(nc3, ims3, core_ids=list(range(8))).results

    out = np.zeros((2, 8192, DM), np.float32)
    for c in range(8):
        b, q = c // 4, c % 4
        out[b, q * T:(q + 1) * T] = np.asarray(res3[c]["o_out"])
    return out

